# version 37 (final)
# baseline (speedup 1.0000x reference)
import contextlib

import numpy as np
import ml_dtypes

import concourse.bass as bass
import concourse.mybir as mybir
from concourse.bass_utils import run_bass_kernel_spmd

F32 = mybir.dt.float32
BF16 = mybir.dt.bfloat16
AF = mybir.ActivationFunctionType
ALU = mybir.AluOpType

EPS = 1e-5
NCORES = 8
D = 4096
TOK = 1024
NT = TOK // 128
NOT_ = D // 512
IC = D // 128
NW = 16
WPC = (IC * 512) // NW
NWS = 5
NTO = 16

_NC_CACHE = {}


def build_nc():
    nc = bass.Bass("TRN2", target_bir_lowering=False)

    x_h = nc.dram_tensor("x", [TOK, D], F32, kind="ExternalInput")
    w_h = nc.dram_tensor("w2d", [NOT_ * 128, IC * 512], F32, kind="ExternalInput")
    br_h = nc.dram_tensor("biasrow", [1, D], F32, kind="ExternalInput")
    kk_h = nc.dram_tensor("kk", [128, 1], F32, kind="ExternalInput")
    aa_h = nc.dram_tensor("aa", [128, 1], F32, kind="ExternalInput")
    id_h = nc.dram_tensor("ident", [128, 128], BF16, kind="ExternalInput")
    y_h = nc.dram_tensor("y", [TOK, D], F32, kind="ExternalOutput")

    es = contextlib.ExitStack()
    sb = nc.sbuf_tensor
    h2T = es.enter_context(sb("h2T", [128, NT * D], BF16))
    wp = [es.enter_context(sb(f"wp{i}", [128, IC * 512], BF16)) for i in range(2)]
    xs = [es.enter_context(sb(f"xs{i}", [128, D], F32)) for i in range(2)]
    h2b = es.enter_context(sb("h2b", [128, D], BF16))
    wr = es.enter_context(sb("wring", [128, NWS * WPC], F32))
    kk = es.enter_context(sb("kk_sb", [128, 1], F32))
    aa = es.enter_context(sb("aa_sb", [128, 1], F32))
    ident = es.enter_context(sb("ident_sb", [128, 128], BF16))
    stats = es.enter_context(sb("stats", [128, 96], F32))
    mv = es.enter_context(sb("mv", [128, 2], F32))
    sc = es.enter_context(sb("sc", [128, 8], F32))
    s1a = es.enter_context(sb("s1a", [128, NT], F32))
    b1a = es.enter_context(sb("b1a", [128, NT], F32))
    rbb = [es.enter_context(sb(f"rb{i}", [128, 512], F32)) for i in range(4)]
    outs = [es.enter_context(sb(f"outs{i}", [128, 512], F32)) for i in range(4)]
    ps_acc = [es.enter_context(nc.psum_tensor(f"ps_acc{i}", [128, 512], F32)) for i in range(4)]
    ps_tr = [es.enter_context(nc.psum_tensor(f"ps_tr{i}", [128, 512], F32)) for i in range(2)]
    ps_wu = es.enter_context(nc.psum_tensor("ps_wu", [128, 512], F32))

    sem = nc.semaphore
    s_const = es.enter_context(sem("s_const"))
    s_xin = [es.enter_context(sem(f"s_xin{i}")) for i in range(2)]
    s_rb = [es.enter_context(sem(f"s_rb{i}")) for i in range(4)]
    s_ywr = [es.enter_context(sem(f"s_ywr{i}")) for i in range(4)]
    s_wl = [es.enter_context(sem(f"s_wl{i}")) for i in range(NWS)]
    s_ba = [es.enter_context(sem(f"s_ba{i}")) for i in range(4)]
    s_bb = [es.enter_context(sem(f"s_bb{i}")) for i in range(4)]
    s_v = es.enter_context(sem("s_v"))
    s_sc = es.enter_context(sem("s_sc"))
    s_evT = es.enter_context(sem("s_evT"))
    s_trT = es.enter_context(sem("s_trT"))
    s_wp = es.enter_context(sem("s_wp"))
    s_grp = es.enter_context(sem("s_grp"))
    s_ev = es.enter_context(sem("s_ev"))

    block = es.enter_context(nc.Block())

    GROUPS = [(ot, t) for ot in range(NOT_) for t in range(NT)]
    NG = len(GROUPS)
    last_g_of_ot = {ot: max(g for g, (o, _) in enumerate(GROUPS) if o == ot)
                    for ot in range(NOT_)}
    evac_after_tile = {t: [] for t in range(NT)}
    late_evacs = []
    prev_tile = 0
    for g, (ot, t) in enumerate(GROUPS[:8]):
        at = max(t + 2, prev_tile)
        if at <= NT - 1:
            evac_after_tile[at].append(g)
            prev_tile = at
        else:
            late_evacs.append(g)
    late_evacs += [g for g in range(8, NG)]


    @block.sync
    def _(sync):
        sync.dma_start(kk[:, :], kk_h[:, :]).then_inc(s_const, 16)
        sync.dma_start(aa[:, :], aa_h[:, :]).then_inc(s_const, 16)
        sync.dma_start(ident[:, :], id_h[:, :]).then_inc(s_const, 16)
        sync.dma_start(xs[0][:, :], x_h[0:128, :]).then_inc(s_xin[0], 16)
        sync.dma_start(xs[1][:, :], x_h[128:256, :]).then_inc(s_xin[1], 16)

        def load_x(t):
            sync.wait_ge(s_sc, 2 * (t - 2) + 2)
            sync.wait_ge(s_v, 4 * (t - 2) + 4)
            sync.dma_start(xs[t % 2][:, :], x_h[t * 128:(t + 1) * 128, :]).then_inc(s_xin[t % 2], 16)

        def rb(g):
            ot, t = GROUPS[g]
            sync.dma_start(rbb[g % 4][:, :],
                           x_h[t * 128:(t + 1) * 128, ot * 512:(ot + 1) * 512]).then_inc(s_rb[g % 4], 16)

        def finish(og):
            oot, ott = GROUPS[og]
            sync.wait_ge(s_ba[og % 4], 16 * (og // 4 + 1))
            sync.dma_start(y_h[ott * 128:(ott + 1) * 128, oot * 512:(oot + 1) * 512],
                           outs[og % 4][:, :]).then_inc(s_ywr[og % 4], 16)

        for g in range(4):
            rb(g)
        load_x(2)
        load_x(3)
        for g in range(4, NG):
            if g <= 7:
                load_x(g)
            sync.wait_ge(s_ev, g - 3)
            finish(g - 4)
            rb(g)
            if g >= NG - 2:
                sync.wait_ge(s_ev, g - 1)
                finish(g - 2)
        for og in range(NG - 2, NG):
            sync.wait_ge(s_ev, og + 1)
            finish(og)
        for i in range(4):
            sync.wait_ge(s_ywr[i], 16 * (NG // 4))

    @block.vector
    def _(vector):
        def emit_evac(g):
            ot, t = GROUPS[g]
            vector.wait_ge(s_grp, g + 1)
            vector.wait_ge(s_rb[g % 4], 16 * (g // 4 + 1))
            if g >= 4:
                vector.wait_ge(s_ywr[g % 4], 16 * ((g - 4) // 4 + 1))
            vector.scalar_tensor_tensor(outs[g % 4][:, :], rbb[g % 4][:, :],
                                        s1a[:, t:t + 1], ps_acc[g % 4][:, :],
                                        ALU.mult, ALU.add)
            vector.drain()
            vector.tensor_scalar(outs[g % 4][:, :], outs[g % 4][:, :],
                                 b1a[:, t:t + 1], None, ALU.add).then_inc(s_ev, 1)

        vector.wait_ge(s_const, 48)
        for t in range(NT):
            p = t % 2
            vector.wait_ge(s_xin[p], 16 * (t // 2 + 1))
            for j in range(8):
                vector.bn_stats(stats[:, 48 * p + 6 * j:48 * p + 6 * j + 6],
                                xs[p][:, 512 * j:512 * (j + 1)])
            vector.drain()
            vector.bn_aggr(mv[:, 0:2], stats[:, 48 * p:48 * p + 48])
            vector.drain()
            vector.tensor_scalar(sc[:, 1:2], mv[:, 1:2], EPS, None, ALU.add).then_inc(s_v, 1)
            vector.drain()
            vector.tensor_scalar(sc[:, 0:1], sc[:, 1:2], 0.5, 0.5, ALU.mult, ALU.add)
            vector.drain()
            vector.reciprocal(sc[:, 2:3], sc[:, 0:1])
            vector.drain()
            for _ in range(2):
                vector.tensor_scalar(sc[:, 3:4], sc[:, 2:3], sc[:, 2:3], sc[:, 1:2],
                                     ALU.mult, ALU.mult)
                vector.drain()
                vector.tensor_scalar(sc[:, 3:4], sc[:, 3:4], -0.5, 1.5, ALU.mult, ALU.add)
                vector.drain()
                vector.tensor_mul(sc[:, 2:3], sc[:, 2:3], sc[:, 3:4])
                vector.drain()
            vector.tensor_scalar(s1a[:, t:t + 1], sc[:, 2:3], 2.0, 1.0, ALU.mult, ALU.add)
            vector.scalar_tensor_tensor(b1a[:, t:t + 1], mv[:, 0:1], -2.0, sc[:, 2:3],
                                        ALU.mult, ALU.mult)
            vector.drain()
            vector.tensor_scalar(sc[:, 3:4], mv[:, 1:2], s1a[:, t:t + 1], s1a[:, t:t + 1],
                                 ALU.mult, ALU.mult)
            vector.reciprocal(sc[:, 4:5], s1a[:, t:t + 1])
            vector.drain()
            vector.tensor_mul(sc[:, 2:3], sc[:, 2:3], sc[:, 4:5])
            vector.tensor_scalar(sc[:, 1:2], sc[:, 3:4], EPS, None, ALU.add)
            vector.drain()
            for _ in range(1):
                vector.tensor_scalar(sc[:, 3:4], sc[:, 2:3], sc[:, 2:3], sc[:, 1:2],
                                     ALU.mult, ALU.mult)
                vector.drain()
                vector.tensor_scalar(sc[:, 3:4], sc[:, 3:4], -0.5, 1.5, ALU.mult, ALU.add)
                vector.drain()
                vector.tensor_mul(sc[:, 2:3], sc[:, 2:3], sc[:, 3:4])
                vector.drain()
            vector.tensor_mul(sc[:, 3:4], sc[:, 2:3], aa[:, :])
            vector.drain()
            if t > 0:
                vector.wait_ge(s_sc, 2 * t)
            vector.tensor_mul(sc[:, 6:7], s1a[:, t:t + 1], sc[:, 3:4])
            vector.scalar_tensor_tensor(sc[:, 7:8], mv[:, 0:1], -1.0, s1a[:, t:t + 1],
                                        ALU.mult, ALU.mult)
            vector.drain()
            vector.tensor_mul(sc[:, 7:8], sc[:, 7:8], sc[:, 3:4]).then_inc(s_v, 1)
            vector.drain()
            if t > 0:
                vector.wait_ge(s_trT, 8 * t)
            vector.tensor_scalar(h2b[:, 0:1024], xs[p][:, 0:1024], sc[:, 6:7],
                                 sc[:, 7:8], ALU.mult, ALU.add).then_inc(s_v, 1)
            vector.tensor_scalar(h2b[:, 1024:2048], xs[p][:, 1024:2048], sc[:, 6:7],
                                 sc[:, 7:8], ALU.mult, ALU.add).then_inc(s_v, 1)
            for g in evac_after_tile[t]:
                emit_evac(g)
        for g in late_evacs:
            emit_evac(g)

    @block.scalar
    def _(scalar):
        wl = [0]

        def issue_w():
            w = wl[0]
            while w < NOT_ * NW and w % NWS == 4:
                w += 1
            if w >= NOT_ * NW:
                return
            ot, j = divmod(w, NW)
            sl = w % NWS
            scalar.dma_start(wr[:, sl * WPC:(sl + 1) * WPC],
                             w_h[ot * 128:(ot + 1) * 128, j * WPC:(j + 1) * WPC]).then_inc(s_wl[sl], 16)
            wl[0] = w + 1

        def tanh_batch(ot, lo=0, hi=NTO):
            for jop in range(lo, hi):
                w = ot * NW + jop
                sl = w % NWS
                scalar.wait_ge(s_wl[sl], 16 * (w // NWS + 1))
                scalar.activation(wp[ot % 2][:, jop * WPC:(jop + 1) * WPC],
                                  wr[:, sl * WPC:(sl + 1) * WPC],
                                  AF.Tanh, bias=0.0, scale=kk[:, :]).then_inc(s_wp, 1)
                scalar.drain()
                if sl != 4:
                    issue_w()

        def tile_block(t):
            scalar.wait_ge(s_v, 4 * t + 2)
            if t > 0:
                scalar.wait_ge(s_trT, 8 * t)
            scalar.activation(h2b[:, 2048:3072], xs[t % 2][:, 2048:3072], AF.Identity,
                              bias=sc[:, 7:8], scale=sc[:, 6:7]).then_inc(s_sc, 1)
            scalar.activation(h2b[:, 3072:D], xs[t % 2][:, 3072:D], AF.Identity,
                              bias=sc[:, 7:8], scale=sc[:, 6:7]).then_inc(s_sc, 1)
            for k in range(8):
                scalar.wait_ge(s_trT, 8 * t + k + 1)
                scalar.copy(h2T[:, t * D + k * 512: t * D + (k + 1) * 512],
                            ps_tr[k % 2][:, :]).then_inc(s_evT, 1)

        scalar.wait_ge(s_const, 48)
        for _ in range(4):
            issue_w()
        tile_block(0)
        tanh_batch(0)
        for t in range(1, NT):
            tile_block(t)
            if 1 <= t <= 4:
                tanh_batch(1, 4 * (t - 1), 4 * t)
        for ot in range(2, NOT_):
            scalar.wait_ge(s_grp, last_g_of_ot[ot - 2] + 1)
            tanh_batch(ot)

    @block.tensor
    def _(tensor):
        tensor.wait_ge(s_const, 48)
        for _ in range(60):
            tensor.matmul(ps_wu[:, 0:128], ident[:, :], ident[:, :], start=True, stop=True)
        for j in range(NWS):
            tensor.wait_ge(s_wl[j], 16)
            for _ in range(3):
                tensor.matmul(ps_wu[:, 0:128], ident[:, :], ident[:, :], start=True, stop=True)
        trg_ctr = [0]

        def emit_tg(t):
            for k in range(8):
                if k == 0:
                    tensor.wait_ge(s_v, 4 * t + 3)
                elif k == 2:
                    tensor.wait_ge(s_v, 4 * t + 4)
                elif k == 4:
                    tensor.wait_ge(s_sc, 2 * t + 1)
                elif k == 6:
                    tensor.wait_ge(s_sc, 2 * t + 2)
                trg = trg_ctr[0]
                if trg >= 2:
                    tensor.wait_ge(s_evT, trg - 1)
                for q in range(4):
                    ic = 4 * k + q
                    mm = tensor.matmul(ps_tr[trg % 2][:, q * 128:(q + 1) * 128],
                                       h2b[:, ic * 128:(ic + 1) * 128], ident[:, :],
                                       start=True, stop=True)
                mm.then_inc(s_trT, 1)
                trg_ctr[0] += 1

        emitted_t = set()
        for g, (ot, t) in enumerate(GROUPS):
            first_of_ot = (t <= 1)
            if t not in emitted_t:
                emit_tg(t)
                emitted_t.add(t)
            if not first_of_ot:
                tensor.wait_ge(s_wp, NTO * (ot + 1))
            if ot > 0:
                tensor.wait_ge(s_evT, 8 * (t + 1))
            if g >= 4:
                tensor.wait_ge(s_ev, g - 3)
            for ic in range(IC):
                if first_of_ot and ic % 8 == 0:
                    tensor.wait_ge(s_wp, NTO * ot + ic // 2 + 4)
                if ot == 0 and ic % 4 == 0:
                    tensor.wait_ge(s_evT, 8 * t + ic // 4 + 1)
                mm = tensor.matmul(ps_acc[g % 4][:, :],
                                   h2T[:, t * D + ic * 128: t * D + (ic + 1) * 128],
                                   wp[ot % 2][:, ic * 512:(ic + 1) * 512],
                                   start=(ic == 0), stop=(ic == IC - 1))
            mm.then_inc(s_grp, 1)

    @block.gpsimd
    def _(gpsimd):
        gpsimd.wait_ge(s_const, 48)

        def evens(ot):
            for j in range(NW):
                w = ot * NW + j
                sl = w % NWS
                if sl != 4:
                    continue
                if w >= NWS:
                    gpsimd.wait_ge(s_wp, w - NWS + 1)
                gpsimd.dma_start(wr[:, sl * WPC:(sl + 1) * WPC],
                                 w_h[ot * 128:(ot + 1) * 128,
                                     j * WPC:(j + 1) * WPC]).then_inc(s_wl[sl], 16)

        evens(0)
        evens(1)
        for g in range(NG):
            ot = g // NT
            if g % NT == 0 and ot + 1 >= 2 and ot + 1 < NOT_:
                evens(ot + 1)
            oot, ott = GROUPS[g]
            gpsimd.wait_ge(s_ev, g + 1)
            gpsimd.dma_start(outs[g % 4][:, :],
                             bass.AP(br_h, oot * 512, [[0, 128], [1, 512]]),
                             accum_op=ALU.add).then_inc(s_ba[g % 4], 16)

    es.close()
    return nc


def kernel(L0_attn_x, L0_ffn_weight, L0_ffn_bias, L0_ffn_kk, L0_ffn_aa):
    x = np.asarray(L0_attn_x, np.float32).reshape(8192, D)
    W = np.asarray(L0_ffn_weight, np.float32)
    bias = np.asarray(L0_ffn_bias, np.float32)
    kk_v = np.float32(np.asarray(L0_ffn_kk))
    aa_v = np.float32(np.asarray(L0_ffn_aa))

    w2d = np.ascontiguousarray(
        W.T.reshape(IC, 128, NOT_, 512).transpose(2, 1, 0, 3).reshape(NOT_ * 128, IC * 512))
    biasrow = np.ascontiguousarray(bias.reshape(1, D))
    kk_b = np.full((128, 1), kk_v, np.float32)
    aa_b = np.full((128, 1), aa_v, np.float32)
    ident = np.eye(128, dtype=ml_dtypes.bfloat16)

    if "nc" not in _NC_CACHE:
        _NC_CACHE["nc"] = build_nc()
    nc = _NC_CACHE["nc"]

    in_maps = []
    for c in range(NCORES):
        in_maps.append({
            "x": x[c * TOK:(c + 1) * TOK, :],
            "w2d": w2d,
            "biasrow": biasrow,
            "kk": kk_b,
            "aa": aa_b,
            "ident": ident,
        })
    res = run_bass_kernel_spmd(nc, in_maps, core_ids=list(range(NCORES)))
    out = np.concatenate([np.asarray(r["y"]) for r in res.results], axis=0)
    return out.reshape(1, 8192, D).astype(np.float32, copy=False)


# version 38
# speedup vs baseline: 1.0433x; 1.0433x over previous
import contextlib

import numpy as np
import ml_dtypes

import concourse.bass as bass
import concourse.mybir as mybir
from concourse.bass_utils import run_bass_kernel_spmd

F32 = mybir.dt.float32
BF16 = mybir.dt.bfloat16
AF = mybir.ActivationFunctionType
ALU = mybir.AluOpType

EPS = 1e-5
NCORES = 8
D = 4096
TOK = 1024
NT = TOK // 128
NOT_ = D // 512
IC = D // 128
NW = 16
WPC = (IC * 512) // NW
NWS = 5
NTO = 16

_NC_CACHE = {}


def build_nc():
    nc = bass.Bass("TRN2", target_bir_lowering=False)

    x_h = nc.dram_tensor("x", [TOK, D], F32, kind="ExternalInput")
    w_h = nc.dram_tensor("w2d", [NOT_ * 128, IC * 512], F32, kind="ExternalInput")
    br_h = nc.dram_tensor("biasrow", [1, D], F32, kind="ExternalInput")
    kk_h = nc.dram_tensor("kk", [128, 1], F32, kind="ExternalInput")
    aa_h = nc.dram_tensor("aa", [128, 1], F32, kind="ExternalInput")
    id_h = nc.dram_tensor("ident", [128, 128], BF16, kind="ExternalInput")
    y_h = nc.dram_tensor("y", [TOK, D], F32, kind="ExternalOutput")

    es = contextlib.ExitStack()
    sb = nc.sbuf_tensor
    h2T = es.enter_context(sb("h2T", [128, NT * D], BF16))
    wp = [es.enter_context(sb(f"wp{i}", [128, IC * 512], BF16)) for i in range(2)]
    xs = [es.enter_context(sb(f"xs{i}", [128, D], F32)) for i in range(2)]
    h2b = es.enter_context(sb("h2b", [128, D], BF16))
    wr = es.enter_context(sb("wring", [128, NWS * WPC], F32))
    kk = es.enter_context(sb("kk_sb", [128, 1], F32))
    aa = es.enter_context(sb("aa_sb", [128, 1], F32))
    ident = es.enter_context(sb("ident_sb", [128, 128], BF16))
    stats = es.enter_context(sb("stats", [128, 96], F32))
    mv = es.enter_context(sb("mv", [128, 2], F32))
    sc = es.enter_context(sb("sc", [128, 8], F32))
    s1a = es.enter_context(sb("s1a", [128, NT], F32))
    b1a = es.enter_context(sb("b1a", [128, NT], F32))
    rbb = [es.enter_context(sb(f"rb{i}", [128, 512], F32)) for i in range(4)]
    outs = [es.enter_context(sb(f"outs{i}", [128, 512], F32)) for i in range(4)]
    ps_acc = [es.enter_context(nc.psum_tensor(f"ps_acc{i}", [128, 512], F32)) for i in range(4)]
    ps_tr = [es.enter_context(nc.psum_tensor(f"ps_tr{i}", [128, 512], F32)) for i in range(2)]
    ps_wu = es.enter_context(nc.psum_tensor("ps_wu", [128, 512], F32))

    sem = nc.semaphore
    s_const = es.enter_context(sem("s_const"))
    s_xin = [es.enter_context(sem(f"s_xin{i}")) for i in range(2)]
    s_rb = [es.enter_context(sem(f"s_rb{i}")) for i in range(4)]
    s_ywr = [es.enter_context(sem(f"s_ywr{i}")) for i in range(4)]
    s_wl = [es.enter_context(sem(f"s_wl{i}")) for i in range(NWS)]
    s_ba = [es.enter_context(sem(f"s_ba{i}")) for i in range(4)]
    s_bb = [es.enter_context(sem(f"s_bb{i}")) for i in range(4)]
    s_v = es.enter_context(sem("s_v"))
    s_sc = es.enter_context(sem("s_sc"))
    s_evT = es.enter_context(sem("s_evT"))
    s_trT = es.enter_context(sem("s_trT"))
    s_wp = es.enter_context(sem("s_wp"))
    s_grp = es.enter_context(sem("s_grp"))
    s_ev = es.enter_context(sem("s_ev"))

    block = es.enter_context(nc.Block())

    GROUPS = [(ot, t) for ot in range(NOT_) for t in range(NT)]
    NG = len(GROUPS)
    last_g_of_ot = {ot: max(g for g, (o, _) in enumerate(GROUPS) if o == ot)
                    for ot in range(NOT_)}
    evac_after_tile = {t: [] for t in range(NT)}
    late_evacs = []
    prev_tile = 0
    for g, (ot, t) in enumerate(GROUPS[:8]):
        at = max(t + 2, prev_tile)
        if at <= NT - 1:
            evac_after_tile[at].append(g)
            prev_tile = at
        else:
            late_evacs.append(g)
    late_evacs += [g for g in range(8, NG)]


    @block.sync
    def _(sync):
        sync.dma_start(kk[:, :], kk_h[:, :]).then_inc(s_const, 16)
        sync.dma_start(aa[:, :], aa_h[:, :]).then_inc(s_const, 16)
        sync.dma_start(ident[:, :], id_h[:, :]).then_inc(s_const, 16)
        sync.dma_start(xs[0][:, :], x_h[0:128, :]).then_inc(s_xin[0], 16)
        sync.dma_start(xs[1][:, :], x_h[128:256, :]).then_inc(s_xin[1], 16)

        def load_x(t):
            sync.wait_ge(s_sc, 2 * (t - 2) + 2)
            sync.wait_ge(s_v, 4 * (t - 2) + 4)
            sync.dma_start(xs[t % 2][:, :], x_h[t * 128:(t + 1) * 128, :]).then_inc(s_xin[t % 2], 16)

        def rb(g):
            ot, t = GROUPS[g]
            sync.dma_start(rbb[g % 4][:, :],
                           x_h[t * 128:(t + 1) * 128, ot * 512:(ot + 1) * 512]).then_inc(s_rb[g % 4], 16)

        def finish(og):
            oot, ott = GROUPS[og]
            sync.wait_ge(s_ba[og % 4], 16 * (og // 4 + 1))
            sync.dma_start(y_h[ott * 128:(ott + 1) * 128, oot * 512:(oot + 1) * 512],
                           outs[og % 4][:, :]).then_inc(s_ywr[og % 4], 16)

        for g in range(4):
            rb(g)
        load_x(2)
        load_x(3)
        for g in range(4, NG):
            if g <= 7:
                load_x(g)
            sync.wait_ge(s_ev, g - 3)
            finish(g - 4)
            rb(g)
            if g >= NG - 2:
                sync.wait_ge(s_ev, g - 1)
                finish(g - 2)
        for og in range(NG - 2, NG):
            sync.wait_ge(s_ev, og + 1)
            finish(og)
        for i in range(4):
            sync.wait_ge(s_ywr[i], 16 * (NG // 4))

    @block.vector
    def _(vector):
        def emit_evac(g):
            ot, t = GROUPS[g]
            vector.wait_ge(s_grp, g + 1)
            vector.wait_ge(s_rb[g % 4], 16 * (g // 4 + 1))
            if g >= 4:
                vector.wait_ge(s_ywr[g % 4], 16 * ((g - 4) // 4 + 1))
            vector.scalar_tensor_tensor(outs[g % 4][:, :], rbb[g % 4][:, :],
                                        s1a[:, t:t + 1], ps_acc[g % 4][:, :],
                                        ALU.mult, ALU.add)
            vector.drain()
            vector.tensor_scalar(outs[g % 4][:, :], outs[g % 4][:, :],
                                 b1a[:, t:t + 1], None, ALU.add).then_inc(s_ev, 1)

        vector.wait_ge(s_const, 48)
        for t in range(NT):
            p = t % 2
            vector.wait_ge(s_xin[p], 16 * (t // 2 + 1))
            for j in range(8):
                vector.bn_stats(stats[:, 48 * p + 6 * j:48 * p + 6 * j + 6],
                                xs[p][:, 512 * j:512 * (j + 1)])
            vector.drain()
            vector.bn_aggr(mv[:, 0:2], stats[:, 48 * p:48 * p + 48])
            vector.drain()
            vector.tensor_scalar(sc[:, 1:2], mv[:, 1:2], EPS, None, ALU.add).then_inc(s_v, 1)
            vector.drain()
            vector.tensor_scalar(sc[:, 0:1], sc[:, 1:2], 0.5, 0.5, ALU.mult, ALU.add)
            vector.drain()
            vector.reciprocal(sc[:, 2:3], sc[:, 0:1])
            vector.drain()
            for _ in range(2):
                vector.tensor_scalar(sc[:, 3:4], sc[:, 2:3], sc[:, 2:3], sc[:, 1:2],
                                     ALU.mult, ALU.mult)
                vector.drain()
                vector.tensor_scalar(sc[:, 3:4], sc[:, 3:4], -0.5, 1.5, ALU.mult, ALU.add)
                vector.drain()
                vector.tensor_mul(sc[:, 2:3], sc[:, 2:3], sc[:, 3:4])
                vector.drain()
            vector.tensor_scalar(s1a[:, t:t + 1], sc[:, 2:3], 2.0, 1.0, ALU.mult, ALU.add)
            vector.scalar_tensor_tensor(b1a[:, t:t + 1], mv[:, 0:1], -2.0, sc[:, 2:3],
                                        ALU.mult, ALU.mult)
            vector.drain()
            vector.tensor_scalar(sc[:, 3:4], mv[:, 1:2], s1a[:, t:t + 1], s1a[:, t:t + 1],
                                 ALU.mult, ALU.mult)
            vector.reciprocal(sc[:, 4:5], s1a[:, t:t + 1])
            vector.drain()
            vector.tensor_mul(sc[:, 2:3], sc[:, 2:3], sc[:, 4:5])
            vector.tensor_scalar(sc[:, 1:2], sc[:, 3:4], EPS, None, ALU.add)
            vector.drain()
            for _ in range(1):
                vector.tensor_scalar(sc[:, 3:4], sc[:, 2:3], sc[:, 2:3], sc[:, 1:2],
                                     ALU.mult, ALU.mult)
                vector.drain()
                vector.tensor_scalar(sc[:, 3:4], sc[:, 3:4], -0.5, 1.5, ALU.mult, ALU.add)
                vector.drain()
                vector.tensor_mul(sc[:, 2:3], sc[:, 2:3], sc[:, 3:4])
                vector.drain()
            vector.tensor_mul(sc[:, 3:4], sc[:, 2:3], aa[:, :])
            vector.drain()
            if t > 0:
                vector.wait_ge(s_sc, 2 * t)
            vector.tensor_mul(sc[:, 6:7], s1a[:, t:t + 1], sc[:, 3:4])
            vector.scalar_tensor_tensor(sc[:, 7:8], mv[:, 0:1], -1.0, s1a[:, t:t + 1],
                                        ALU.mult, ALU.mult)
            vector.drain()
            vector.tensor_mul(sc[:, 7:8], sc[:, 7:8], sc[:, 3:4]).then_inc(s_v, 1)
            vector.drain()
            if t > 0:
                vector.wait_ge(s_trT, 8 * t)
            vector.tensor_scalar(h2b[:, 0:1024], xs[p][:, 0:1024], sc[:, 6:7],
                                 sc[:, 7:8], ALU.mult, ALU.add).then_inc(s_v, 1)
            vector.tensor_scalar(h2b[:, 1024:2048], xs[p][:, 1024:2048], sc[:, 6:7],
                                 sc[:, 7:8], ALU.mult, ALU.add).then_inc(s_v, 1)
            for g in evac_after_tile[t]:
                emit_evac(g)
        for g in late_evacs:
            emit_evac(g)

    @block.scalar
    def _(scalar):
        wl = [0]

        def issue_w():
            w = wl[0]
            while w < NOT_ * NW and w % NWS == 4:
                w += 1
            if w >= NOT_ * NW:
                return
            ot, j = divmod(w, NW)
            sl = w % NWS
            scalar.dma_start(wr[:, sl * WPC:(sl + 1) * WPC],
                             w_h[ot * 128:(ot + 1) * 128, j * WPC:(j + 1) * WPC]).then_inc(s_wl[sl], 16)
            wl[0] = w + 1

        def tanh_batch(ot, lo=0, hi=NTO):
            for jop in range(lo, hi):
                w = ot * NW + jop
                sl = w % NWS
                scalar.wait_ge(s_wl[sl], 16 * (w // NWS + 1))
                scalar.activation(wp[ot % 2][:, jop * WPC:(jop + 1) * WPC],
                                  wr[:, sl * WPC:(sl + 1) * WPC],
                                  AF.Tanh, bias=0.0, scale=kk[:, :]).then_inc(s_wp, 1)
                scalar.drain()
                if sl != 4:
                    issue_w()

        def tile_block(t):
            scalar.wait_ge(s_v, 4 * t + 2)
            if t > 0:
                scalar.wait_ge(s_trT, 8 * t)
            scalar.activation(h2b[:, 2048:3072], xs[t % 2][:, 2048:3072], AF.Identity,
                              bias=sc[:, 7:8], scale=sc[:, 6:7]).then_inc(s_sc, 1)
            scalar.activation(h2b[:, 3072:D], xs[t % 2][:, 3072:D], AF.Identity,
                              bias=sc[:, 7:8], scale=sc[:, 6:7]).then_inc(s_sc, 1)
            for k in range(8):
                scalar.wait_ge(s_trT, 8 * t + k + 1)
                scalar.copy(h2T[:, t * D + k * 512: t * D + (k + 1) * 512],
                            ps_tr[k % 2][:, :]).then_inc(s_evT, 1)

        scalar.wait_ge(s_const, 48)
        for _ in range(4):
            issue_w()
        tile_block(0)
        tanh_batch(0)
        for t in range(1, NT):
            tile_block(t)
            if 1 <= t <= 4:
                tanh_batch(1, 4 * (t - 1), 4 * t)
        for ot in range(2, NOT_):
            scalar.wait_ge(s_grp, last_g_of_ot[ot - 2] + 1)
            tanh_batch(ot)

    @block.tensor
    def _(tensor):
        tensor.wait_ge(s_const, 48)
        for _ in range(60):
            tensor.matmul(ps_wu[:, 0:128], ident[:, :], ident[:, :], start=True, stop=True)
        for j in range(NWS):
            tensor.wait_ge(s_wl[j], 16)
            for _ in range(3):
                tensor.matmul(ps_wu[:, 0:128], ident[:, :], ident[:, :], start=True, stop=True)
        trg_ctr = [0]

        def emit_tg(t):
            for k in range(8):
                if k == 0:
                    tensor.wait_ge(s_v, 4 * t + 3)
                elif k == 2:
                    tensor.wait_ge(s_v, 4 * t + 4)
                elif k == 4:
                    tensor.wait_ge(s_sc, 2 * t + 1)
                elif k == 6:
                    tensor.wait_ge(s_sc, 2 * t + 2)
                trg = trg_ctr[0]
                if trg >= 2:
                    tensor.wait_ge(s_evT, trg - 1)
                for q in range(4):
                    ic = 4 * k + q
                    mm = tensor.matmul(ps_tr[trg % 2][:, q * 128:(q + 1) * 128],
                                       h2b[:, ic * 128:(ic + 1) * 128], ident[:, :],
                                       start=True, stop=True)
                mm.then_inc(s_trT, 1)
                trg_ctr[0] += 1

        emitted_t = set()
        for g, (ot, t) in enumerate(GROUPS):
            first_of_ot = (t <= 1)
            if t not in emitted_t:
                emit_tg(t)
                emitted_t.add(t)
            if not first_of_ot:
                tensor.wait_ge(s_wp, NTO * (ot + 1))
            if ot > 0:
                tensor.wait_ge(s_evT, 8 * (t + 1))
            if g >= 4:
                tensor.wait_ge(s_ev, g - 3)
            for ic in range(IC):
                if first_of_ot and ic % 4 == 0:
                    tensor.wait_ge(s_wp, NTO * ot + ic // 2 + 2)
                if ot == 0 and ic % 4 == 0:
                    tensor.wait_ge(s_evT, 8 * t + ic // 4 + 1)
                mm = tensor.matmul(ps_acc[g % 4][:, :],
                                   h2T[:, t * D + ic * 128: t * D + (ic + 1) * 128],
                                   wp[ot % 2][:, ic * 512:(ic + 1) * 512],
                                   start=(ic == 0), stop=(ic == IC - 1))
            mm.then_inc(s_grp, 1)

    @block.gpsimd
    def _(gpsimd):
        gpsimd.wait_ge(s_const, 48)

        def evens(ot):
            for j in range(NW):
                w = ot * NW + j
                sl = w % NWS
                if sl != 4:
                    continue
                if w >= NWS:
                    gpsimd.wait_ge(s_wp, w - NWS + 1)
                gpsimd.dma_start(wr[:, sl * WPC:(sl + 1) * WPC],
                                 w_h[ot * 128:(ot + 1) * 128,
                                     j * WPC:(j + 1) * WPC]).then_inc(s_wl[sl], 16)

        evens(0)
        evens(1)
        for g in range(NG):
            ot = g // NT
            if g % NT == 0 and ot + 1 >= 2 and ot + 1 < NOT_:
                evens(ot + 1)
            oot, ott = GROUPS[g]
            gpsimd.wait_ge(s_ev, g + 1)
            gpsimd.dma_start(outs[g % 4][:, :],
                             bass.AP(br_h, oot * 512, [[0, 128], [1, 512]]),
                             accum_op=ALU.add).then_inc(s_ba[g % 4], 16)

    es.close()
    return nc


def kernel(L0_attn_x, L0_ffn_weight, L0_ffn_bias, L0_ffn_kk, L0_ffn_aa):
    x = np.asarray(L0_attn_x, np.float32).reshape(8192, D)
    W = np.asarray(L0_ffn_weight, np.float32)
    bias = np.asarray(L0_ffn_bias, np.float32)
    kk_v = np.float32(np.asarray(L0_ffn_kk))
    aa_v = np.float32(np.asarray(L0_ffn_aa))

    w2d = np.ascontiguousarray(
        W.T.reshape(IC, 128, NOT_, 512).transpose(2, 1, 0, 3).reshape(NOT_ * 128, IC * 512))
    biasrow = np.ascontiguousarray(bias.reshape(1, D))
    kk_b = np.full((128, 1), kk_v, np.float32)
    aa_b = np.full((128, 1), aa_v, np.float32)
    ident = np.eye(128, dtype=ml_dtypes.bfloat16)

    if "nc" not in _NC_CACHE:
        _NC_CACHE["nc"] = build_nc()
    nc = _NC_CACHE["nc"]

    in_maps = []
    for c in range(NCORES):
        in_maps.append({
            "x": x[c * TOK:(c + 1) * TOK, :],
            "w2d": w2d,
            "biasrow": biasrow,
            "kk": kk_b,
            "aa": aa_b,
            "ident": ident,
        })
    res = run_bass_kernel_spmd(nc, in_maps, core_ids=list(range(NCORES)))
    out = np.concatenate([np.asarray(r["y"]) for r in res.results], axis=0)
    return out.reshape(1, 8192, D).astype(np.float32, copy=False)
